# version 06
# baseline (speedup 1.0000x reference)
import numpy as np
import ml_dtypes
from contextlib import ExitStack
import concourse.bass as bass
import concourse.mybir as mybir
from concourse.bass_utils import run_bass_kernel_spmd

F32 = mybir.dt.float32
BF16 = mybir.dt.bfloat16
AF = mybir.ActivationFunctionType
ALU = mybir.AluOpType
AX = mybir.AxisListType

B, S, D = 2, 8192, 2048
TCORE = 2048
CH = 1024
NCH = TCORE // CH
NT = D // 128
NI = 11
NE = 9
S2 = 512
NS = CH // S2
DI = 1408
DFF = 8192
NFF = DFF // 128
EPS_RMS = 1.1920929e-07
LN_EPS = 1e-5
RECIP_D = 1.0 / 2048.0
NTT = CH // 128



class Sem:
    def __init__(self, stack, nc, name):
        self.h = stack.enter_context(nc.semaphore(name))
        self.n = 0

    def inc(self, amt=1):
        self.n += amt
        return self.n


class Ctx:
    pass


def make_ctx(nc, stack):
    c = Ctx()
    c.nc = nc
    c.stack = stack
    c.ct = Sem(stack, nc, "ct")
    c.cs = Sem(stack, nc, "cs")
    c.cv = Sem(stack, nc, "cv")
    c._dsems = {}
    c.steps = {e: [] for e in ("tensor", "vector", "scalar", "gpsimd", "sync")}
    c.waited = {e: {} for e in c.steps}
    return c


def dsem(c, name):
    if name not in c._dsems:
        c._dsems[name] = Sem(c.stack, c.nc, f"dma_{name}")
    return c._dsems[name]


def _record(c, eng_name, sem, fn, wait, inc):
    m = sem.inc(inc) if inc else None

    def run(eng, fn=fn, wait=tuple(wait), inc=inc, sem=sem, eng_name=eng_name):
        seen = c.waited[eng_name]
        for s, val in wait:
            v = val() if callable(val) else val
            if v and v > seen.get(s.h.num, -1):
                eng.wait_ge(s.h, v)
                seen[s.h.num] = v
        last = fn(eng)
        if inc:
            last.then_inc(sem.h, inc)
    c.steps[eng_name].append(run)
    return m


def T(c, fn, wait=(), inc=1):
    return _record(c, "tensor", c.ct, fn, wait, inc)


def V(c, fn, wait=(), inc=1):
    return _record(c, "vector", c.cv, fn, wait, inc)


def SC(c, fn, wait=(), inc=1):
    return _record(c, "scalar", c.cs, fn, wait, inc)


def G(c, sem_name, fn, wait=(), inc=16):
    sem = dsem(c, sem_name)
    return (sem, _record(c, "gpsimd", sem, fn, wait, inc))


def Y(c, sem_name, fn, wait=(), inc=16):
    sem = dsem(c, sem_name)
    return (sem, _record(c, "sync", sem, fn, wait, inc))


def replay(c, block):
    for name in ("gpsimd", "sync", "tensor", "vector", "scalar"):
        steps = c.steps[name]
        getattr(block, name)(lambda eng, steps=steps: [f(eng) for f in steps])



_UID = [0]


def SB(nc, name, shape, dtype, **kw):
    _UID[0] += 1
    return nc.sbuf_tensor(f"{name}_{_UID[0]}", shape, dtype, **kw)


def PS(nc, name, shape, dtype=F32):
    _UID[0] += 1
    return nc.psum_tensor(f"{name}_{_UID[0]}", shape, dtype)

def sbuf(c, name, shape, dtype, side=None):
    kw = {} if side is None else {"side": side}
    return c.stack.enter_context(SB(c.nc, name, shape, dtype, **kw))


def _rearr_rows(ap_2d, row0, nrow_tiles, col0, ncols):
    sub = ap_2d[row0:row0 + nrow_tiles * 128, col0:col0 + ncols]
    return sub.rearrange("(a p) c -> p a c", p=128)


def _sb3(tile_ap, ntiles):
    return tile_ap.rearrange("p (a c) -> p a c", a=ntiles)



def emit_consts(c):
    c.onesc = sbuf(c, "onesc", [128, 1], F32)
    c.onesk1 = sbuf(c, "onesk1", [1, 128], F32)
    c.ones_t = sbuf(c, "ones_t", [128, CH], F32)
    c.c_lneps = sbuf(c, "c_lneps", [128, 1], F32)
    c.c_rmseps = sbuf(c, "c_rmseps", [128, 1], F32)
    c.c_mhalf = sbuf(c, "c_mhalf", [128, 1], F32)
    c.c_m1 = sbuf(c, "c_m1", [128, 1], F32)

    def f(v):
        v.memset(c.onesc[:], 1.0)
        v.memset(c.onesk1[:], 1.0)
        v.memset(c.ones_t[:], 1.0)
        v.memset(c.c_lneps[:], LN_EPS)
        v.memset(c.c_rmseps[:], EPS_RMS)
        v.memset(c.c_mhalf[:], -0.5)
        return v.memset(c.c_m1[:], -1.0)
    c.m_consts = V(c, f)


def emit_ln(c, x_sb, attn, resid_scale, xout_bf, deps):
    nc = c.nc
    with ExitStack() as ph:
        E = ph.enter_context
        mean_r = E(SB(nc, "ln_mean_r", [1, CH], F32))
        sq_r = E(SB(nc, "ln_sq_r", [1, CH], F32))
        var_r = E(SB(nc, "ln_var_r", [1, CH], F32))
        rstd_r = E(SB(nc, "ln_rstd_r", [1, CH], F32))
        tmp_r = rstd_r
        mean_b = E(SB(nc, "ln_mean_b", [128, CH], F32))
        rstd_b = E(SB(nc, "ln_rstd_b", [128, CH], F32))
        if attn:
            re_r = E(SB(nc, "ln_re_r", [1, CH], F32))
            rec1_r = E(SB(nc, "ln_rec1_r", [1, CH], F32))
            reff_b = E(SB(nc, "ln_reff_b", [128, CH], F32))
        scrA = E(SB(nc, "ln_scrA", [128, CH], F32))
        scrT = E(SB(nc, "ln_scrT", [128, CH], F32))
        x2 = [scrA, scrT]

        with ExitStack() as pp:
            mean_ps = pp.enter_context(PS(nc, "ln_mean_ps", [1, CH], F32))
            sq_ps = pp.enter_context(PS(nc, "ln_sq_ps", [1, CH], F32))

            m_x2 = {}
            m_sq = {}
            for j in range(NT):
                m_x2[j] = V(c, lambda v, j=j: v.tensor_tensor(
                    x2[j % 2][:], x_sb[:, j * CH:(j + 1) * CH],
                    x_sb[:, j * CH:(j + 1) * CH], ALU.mult),
                    wait=(list(deps) if j == 0 else
                          ([(c.ct, lambda j=j: m_sq[j - 2])] if j >= 2 else [])))

            def f_mean(t):
                last = None
                for j in range(NT):
                    for s in range(NS):
                        last = t.matmul(
                            mean_ps[:, s * S2:(s + 1) * S2], c.onesc[:],
                            x_sb[:, j * CH + s * S2: j * CH + (s + 1) * S2],
                            start=(j == 0), stop=(j == NT - 1),
                            skip_group_check=True)
                return last
            T(c, f_mean, wait=list(deps), inc=0)

            for j in range(NT):
                def f_sq(t, j=j):
                    last = None
                    for s in range(NS):
                        last = t.matmul(sq_ps[:, s * S2:(s + 1) * S2], c.onesc[:],
                                        x2[j % 2][:, s * S2:(s + 1) * S2],
                                        start=(j == 0), stop=(j == NT - 1),
                                        skip_group_check=True)
                    return last
                m_sq[j] = T(c, f_sq, wait=[(c.cv, m_x2[j])])

            mr1 = V(c, lambda v: v.tensor_scalar_mul(mean_r[:], mean_ps[:],
                                                     RECIP_D),
                    wait=[(c.ct, m_sq[NT - 1])])
            mr2 = V(c, lambda v: v.tensor_scalar_mul(sq_r[:], sq_ps[:],
                                                     RECIP_D),
                    wait=[(c.cv, mr1)])

            def f_rows(v):
                v.tensor_tensor(tmp_r[:], mean_r[:], mean_r[:], ALU.mult)
                return v.tensor_tensor(var_r[:], sq_r[:], tmp_r[:], ALU.subtract)
            m_var = V(c, f_rows, wait=[(c.cv, mr2)])

        def f_rstd(s):
            s.activation(out=tmp_r[:], in_=var_r[:], func=AF.Ln,
                         bias=c.c_lneps[0:1, :])
            return s.activation(out=rstd_r[:], in_=tmp_r[:], func=AF.Exp,
                                scale=c.c_mhalf[0:1, :])
        m_rstd = SC(c, f_rstd, wait=[(c.cv, m_var)])

        if attn:
            mq1 = V(c, lambda v: v.tensor_scalar_add(re_r[:], var_r[:],
                                                     LN_EPS),
                    wait=[(c.cv, m_var)])

            mq2 = V(c, lambda v: v.reciprocal(rec1_r[:], re_r[:]),
                    wait=[(c.cv, mq1)])
            m_q = V(c, lambda v: v.tensor_tensor(re_r[:], var_r[:], rec1_r[:],
                                                 ALU.mult),
                    wait=[(c.cv, mq2)])

            def f_r1(s):
                s.activation(out=re_r[:], in_=re_r[:], func=AF.Ln,
                             bias=c.c_rmseps[0:1, :])
                return s.activation(out=re_r[:], in_=re_r[:], func=AF.Exp,
                                    scale=c.c_mhalf[0:1, :])
            m_r1 = SC(c, f_r1, wait=[(c.cv, m_q)])

            m_reff = V(c, lambda v: v.tensor_tensor(
                re_r[:], rstd_r[:], re_r[:], ALU.mult),
                wait=[(c.cs, m_r1), (c.cs, m_rstd)])
        else:
            m_reff = None

        with ExitStack() as pp:
            nb = 3 if attn else 2
            bc_ps = [pp.enter_context(PS(nc, f"ln_bc{i}", [128, CH], F32))
                     for i in range(nb)]
            rows = [mean_r, rstd_r] + ([re_r] if attn else [])

            def f_bc(t):
                last = None
                for i in range(nb):
                    for s in range(NS):
                        last = t.matmul(bc_ps[i][:, s * S2:(s + 1) * S2],
                                        c.onesk1[:],
                                        rows[i][:, s * S2:(s + 1) * S2],
                                        start=True, stop=True,
                                        skip_group_check=True)
                return last
            wt = [(c.cs, m_rstd), (c.cv, m_var)]
            if attn:
                wt.append((c.cv, m_reff))
            m_bc = T(c, f_bc, wait=wt)

            def f_bcv(v):
                last = v.tensor_copy(mean_b[:], bc_ps[0][:])
                last = v.tensor_copy(rstd_b[:], bc_ps[1][:])
                if attn:
                    last = v.tensor_copy(reff_b[:], bc_ps[2][:])
                return last
            m_bcv = V(c, f_bcv, wait=[(c.ct, m_bc)])

        m_apply = None
        for j in range(NT):
            def f_pre(v, j=j):
                xj = x_sb[:, j * CH:(j + 1) * CH]
                v.tensor_tensor(scrA[:], xj, mean_b[:], ALU.subtract)
                v.tensor_tensor(scrT[:], scrA[:], rstd_b[:], ALU.mult)
                if attn:
                    return v.tensor_tensor(xout_bf[:, j * CH:(j + 1) * CH],
                                           scrA[:], reff_b[:], ALU.mult)
                return v.tensor_copy(xout_bf[:, j * CH:(j + 1) * CH], scrT[:])
            m_pre = V(c, f_pre, wait=[(c.cv, m_bcv)] if j == 0 else [])
            m_apply = V(c, lambda v, j=j: v.scalar_tensor_tensor(
                x_sb[:, j * CH:(j + 1) * CH], scrT[:], float(resid_scale),
                x_sb[:, j * CH:(j + 1) * CH], ALU.mult, ALU.add),
                wait=[(c.cv, m_pre)])
    return m_apply


def emit_router(c, xln_bf, comb_sb, m_xln, dbg_out=None):
    nc = c.nc
    with ExitStack() as ph:
        E = ph.enter_context
        lgs = E(SB(nc, "rt_lgs", [8, CH], F32))
        w8 = E(SB(nc, "rt_w8", [128, 8], F32))
        e8 = E(SB(nc, "rt_e8", [128, 8], F32))
        msk = E(SB(nc, "rt_msk", [128, 8], F32))
        w2 = E(SB(nc, "rt_w2", [128, 8], F32))
        m1 = E(SB(nc, "rt_m1", [128, 1], F32))
        sumr = E(SB(nc, "rt_sum", [128, 1], F32))
        combt = E(SB(nc, "rt_combt", [128, 8], F32))

        with ExitStack() as pp:
            lg_ps = pp.enter_context(PS(nc, "rt_lg_ps", [8, CH], F32))
            tp_ps = [pp.enter_context(PS(nc, f"rt_tp{i}", [128, 8], F32))
                     for i in range(2)]
            cb_ps = [pp.enter_context(PS(nc, f"rt_cb{i}", [8, 128], F32))
                     for i in range(2)]

            def f_lg(t):
                last = None
                for j in range(NT):
                    for s in range(NS):
                        last = t.matmul(
                            lg_ps[:, s * S2:(s + 1) * S2],
                            c.routerT_bf[:, j * 8:(j + 1) * 8],
                            xln_bf[:, j * CH + s * S2: j * CH + (s + 1) * S2],
                            start=(j == 0), stop=(j == NT - 1),
                            skip_group_check=True)
                return last
            m_lg = T(c, f_lg, wait=[(c.cv, m_xln), c.m_router])

            m_lgs = V(c, lambda v: v.tensor_copy(lgs[:], lg_ps[:]),
                      wait=[(c.ct, m_lg)])
            if dbg_out is not None:
                Y(c, "sc1", lambda y: y.dma_start(out=dbg_out, in_=lgs[:]),
                  wait=[(c.cv, m_lgs)])
            dbg_w8 = c.dbg_w8 if dbg_out is not None else None

            m_w8 = {}
            m_comb = {}
            m_tp = {}
            m_cbt = {}
            m_evac = {}
            for tt in range(NTT):
                m_tp[tt] = T(c, lambda t, tt=tt: t.matmul(
                    tp_ps[tt % 2][:], lgs[:, tt * 128:(tt + 1) * 128],
                    c.ident[0:8, 0:8], start=True, stop=True,
                    skip_group_check=True),
                    wait=[(c.cv, m_lgs), c.m_cst] +
                         ([(c.cv, lambda tt=tt: m_w8[tt - 2])]
                          if tt >= 2 else []))
                m_w8[tt] = V(c, lambda v, tt=tt: v.tensor_copy(
                    w8[:], tp_ps[tt % 2][:]),
                    wait=[(c.ct, m_tp[tt])] +
                         ([(c.cv, lambda tt=tt: m_evac[tt - 1])] if tt >= 1 else []))
                if dbg_w8 is not None and tt < 2:
                    mdbgc = V(c, lambda v, tt=tt: v.tensor_copy(
                        c.dbg_w8_sb[:, tt * 8:(tt + 1) * 8], w8[:]))
                    Y(c, "sc1", lambda y, tt=tt: y.dma_start(
                        out=dbg_w8[:, tt * 8:(tt + 1) * 8],
                        in_=c.dbg_w8_sb[:, tt * 8:(tt + 1) * 8],
                        allow_slow_non_contiguous=True),
                      wait=[(c.cv, mdbgc)])
                mA = V(c, lambda v: v.reduce_max(m1[:], w8[:], axis=AX.X))
                mB = V(c, lambda v: v.tensor_single_scalar(
                    w8[:], w8[:], m1[:], ALU.subtract), wait=[(c.cv, mA)])
                mC = SC(c, lambda s: s.activation(out=e8[:], in_=w8[:],
                                                  func=AF.Exp),
                        wait=[(c.cv, mB)])
                mD = V(c, lambda v: v.reduce_sum(sumr[:], e8[:], axis=AX.X),
                       wait=[(c.cs, mC)])
                mE = V(c, lambda v: v.reciprocal(m1[:], sumr[:]),
                       wait=[(c.cv, mD)])
                mF = V(c, lambda v: v.tensor_single_scalar(
                    w8[:], e8[:], m1[:], ALU.mult), wait=[(c.cv, mE)])
                mG = V(c, lambda v: v.reduce_max(m1[:], w8[:], axis=AX.X),
                       wait=[(c.cv, mF)])
                mH = V(c, lambda v: v.tensor_single_scalar(
                    msk[:], w8[:], m1[:], ALU.is_ge), wait=[(c.cv, mG)])
                mI = V(c, lambda v: v.scalar_tensor_tensor(
                    w2[:], msk[:], -1e30, w8[:], ALU.mult, ALU.add),
                    wait=[(c.cv, mH)])
                mJ = V(c, lambda v: v.reduce_max(m1[:], w2[:], axis=AX.X),
                       wait=[(c.cv, mI)])
                mK = V(c, lambda v: v.tensor_single_scalar(
                    msk[:], w8[:], m1[:], ALU.is_ge), wait=[(c.cv, mJ)])

                mCm = V(c, lambda v: v.tensor_tensor(combt[:], w8[:], msk[:],
                                                     ALU.mult),
                        wait=[(c.cv, mK)] +
                             ([(c.ct, (lambda tt=tt: m_cbt[tt - 1]))]
                              if tt >= 1 else []))
                m_comb[tt] = V(c, lambda v: v.tensor_scalar_mul(
                    combt[:], combt[:], 0.5), wait=[(c.cv, mCm)])

                m_cbt[tt] = T(c, lambda t, tt=tt: t.matmul(
                    cb_ps[tt % 2][:], combt[:], c.ident[:], start=True,
                    stop=True, skip_group_check=True),
                    wait=[(c.cv, m_comb[tt])])
                m_evac[tt] = V(c, lambda v, tt=tt: v.tensor_copy(
                    comb_sb[:, tt * 128:(tt + 1) * 128], cb_ps[tt % 2][:]),
                    wait=[(c.ct, m_cbt[tt])])
            m_all = m_evac[NTT - 1]
    return m_all


def emit_ffn(c, x_sb, resid_scale, deps, tag, experts=None):
    nc = c.nc
    if experts is None:
        experts = list(range(NE))
    with ExitStack() as ph:
        E = ph.enter_context
        xln_bf = E(SB(nc, f"{tag}_xln", [128, NT * CH], BF16))
        he_sb = E(SB(nc, f"{tag}_he", [128, NI * CH], BF16))
        comb_sb = E(SB(nc, f"{tag}_comb", [8, CH], F32))
        comb_b = E(SB(nc, f"{tag}_comb_b", [128, CH], F32))
        sgs = [[E(SB(nc, f"{tag}_sgs{i}{j}", [128, S2], F32)) for j in range(2)]
               for i in range(2)]
        he1 = E(SB(nc, f"{tag}_he1", [128, S2], F32))
        wg_sl = [E(SB(nc, f"{tag}_wg{i}", [128, NT * 128], BF16))
                 for i in range(2)]
        wu_sl = [E(SB(nc, f"{tag}_wu{i}", [128, NT * 128], BF16))
                 for i in range(2)]
        wd_sl = [E(SB(nc, f"{tag}_wd{i}", [128, NI * 128], BF16))
                 for i in range(4)]

        m_apply = emit_ln(c, x_sb, attn=False, resid_scale=resid_scale,
                          xout_bf=xln_bf, deps=deps)
        m_comb = emit_router(c, xln_bf, comb_sb, m_apply)

        m_gu = {}
        m_he = {}
        m_dn = {}
        m_dnev = {}
        m_cb = {}

        with ExitStack() as pp:
            P = [pp.enter_context(PS(nc, f"{tag}_P{i}", [128, CH], F32))
                 for i in range(4)]

            m_wgu = {}
            m_gu = {}
            m_he = {}
            m_dn = {}
            m_dnev = {}
            m_cb = {}
            for e in experts:
                ei = experts.index(e)
                eprev = experts[ei - 1] if ei >= 1 else None
                for ii in range(NI):
                    p = ii % 2
                    prev = ((e, ii - 2) if ii >= 2 else
                            ((eprev, NI - 2 + ii) if eprev is not None else None))
                    wt = ([(c.ct, (lambda prev=prev: m_gu[prev]))] if prev
                          else list(deps))
                    mg = G(c, f"g{p}", lambda g, e=e, ii=ii, p=p: g.dma_start(
                        out=_sb3(wg_sl[p][:], NT),
                        in_=_rearr_rows(c.gateT[e], 0, NT, ii * 128, 128)),
                        wait=wt)
                    mu = G(c, f"u{p}", lambda g, e=e, ii=ii, p=p: g.dma_start(
                        out=_sb3(wu_sl[p][:], NT),
                        in_=_rearr_rows(c.upT[e], 0, NT, ii * 128, 128)),
                        wait=wt)
                    m_wgu[(e, ii)] = [mg, mu]

                if e < 8:
                    def f_cb(t, e=e):
                        last = None
                        for s in range(NS):
                            last = t.matmul(P[2][:, s * S2:(s + 1) * S2],
                                            c.sel8[:, e * 128:(e + 1) * 128],
                                            comb_sb[:, s * S2:(s + 1) * S2],
                                            start=True, stop=True,
                                            skip_group_check=True)
                        return last
                    wt = [(c.cv, m_comb), c.m_cst]
                    if eprev is not None:
                        wt.append((c.cv, (lambda eprev=eprev:
                                          m_dnev[(eprev, NT - 1)])))
                    m_cbt = T(c, f_cb, wait=wt)
                    m_cb[e] = V(c, lambda v: v.tensor_copy(comb_b[:], P[2][:]),
                                wait=[(c.ct, m_cbt)])
                else:
                    prev_he = ((lambda eprev=eprev: m_he[(eprev, NI - 1)])
                               if eprev is not None else None)
                    m_cb[e] = V(c, lambda v: v.memset(comb_b[:], 0.5),
                                wait=([(c.cv, prev_he)] if prev_he else []))

                for ii in range(NI):
                    p = ii % 2
                    gp, up = P[p * 2], P[p * 2 + 1]

                    def f_gu(t, e=e, ii=ii, p=p, gp=gp, up=up):
                        last = None
                        for j in range(NT):
                            for s in range(NS):
                                t.matmul(gp[:, s * S2:(s + 1) * S2],
                                         wg_sl[p][:, j * 128:(j + 1) * 128],
                                         xln_bf[:, j * CH + s * S2:
                                                j * CH + (s + 1) * S2],
                                         start=(j == 0), stop=(j == NT - 1),
                                         skip_group_check=True)
                        for j in range(NT):
                            for s in range(NS):
                                last = t.matmul(up[:, s * S2:(s + 1) * S2],
                                                wu_sl[p][:, j * 128:(j + 1) * 128],
                                                xln_bf[:, j * CH + s * S2:
                                                       j * CH + (s + 1) * S2],
                                                start=(j == 0),
                                                stop=(j == NT - 1),
                                                skip_group_check=True)
                        return last
                    wt = [*m_wgu[(e, ii)], (c.cv, m_apply)]
                    if ii >= 2:
                        wt.append((c.cv, (lambda e=e, ii=ii: m_he[(e, ii - 2)])))
                    elif eprev is not None:
                        wt.append((c.cv, (lambda eprev=eprev, ii=ii:
                                          m_he[(eprev, NI - 1 - ((NI - 1 - ii) % 2))])))
                        wt.append((c.cv, (lambda eprev=eprev, ii=ii:
                                          m_dnev[(eprev, NT - 2 + ii)])))
                    else:
                        wt.append((c.cv, m_comb))
                    if ii == 1 and e < 8:
                        wt.append((c.cv, m_cb[e]))
                    m_gu[(e, ii)] = T(c, f_gu, wait=wt)

                    def f_sig(s, p=p, gp=gp):
                        last = None
                        for s2 in range(NS):
                            last = s.activation(out=sgs[p][s2][:],
                                                in_=gp[:, s2 * S2:(s2 + 1) * S2],
                                                func=AF.Sigmoid)
                        return last
                    wt_s = [(c.ct, m_gu[(e, ii)])]
                    if ii >= 2:
                        wt_s.append((c.cv, (lambda e=e, ii=ii: m_he[(e, ii - 2)])))
                    elif eprev is not None:
                        wt_s.append((c.cv, (lambda eprev=eprev, ii=ii:
                                            m_he[(eprev, NI - 1 - ((NI - 1 - ii) % 2))])))
                    m_sig = SC(c, f_sig, wait=wt_s)

                    def f_he(v, e=e, ii=ii, p=p, up=up):
                        last = None
                        for s2 in range(NS):
                            v.tensor_tensor(he1[:], sgs[p][s2][:],
                                            up[:, s2 * S2:(s2 + 1) * S2],
                                            ALU.mult)
                            last = v.tensor_tensor(
                                he_sb[:, ii * CH + s2 * S2:
                                      ii * CH + (s2 + 1) * S2],
                                he1[:], comb_b[:, s2 * S2:(s2 + 1) * S2],
                                ALU.mult)
                        return last
                    m_he[(e, ii)] = V(c, f_he,
                                      wait=[(c.cs, m_sig), (c.cv, m_cb[e])])

                for oo in range(NT):
                    p = oo % 2
                    p4 = oo % 4
                    dp = P[p * 2]
                    prevd = ((e, oo - 4) if oo >= 4 else
                             ((eprev, NT - 4 + oo) if eprev is not None else None))
                    wt = ([(c.ct, (lambda prevd=prevd: m_dn[prevd]))] if prevd
                          else list(deps))
                    m_wd = G(c, f"d{p4}", lambda g, e=e, oo=oo, p4=p4: g.dma_start(
                        out=_sb3(wd_sl[p4][:], NI),
                        in_=_rearr_rows(c.downT[e], 0, NI, oo * 128, 128)),
                        wait=wt)

                    def f_dn(t, e=e, oo=oo, p4=p4, dp=dp):
                        last = None
                        for ii in range(NI):
                            for s in range(NS):
                                last = t.matmul(dp[:, s * S2:(s + 1) * S2],
                                                wd_sl[p4][:, ii * 128:(ii + 1) * 128],
                                                he_sb[:, ii * CH + s * S2:
                                                      ii * CH + (s + 1) * S2],
                                                start=(ii == 0),
                                                stop=(ii == NI - 1),
                                                skip_group_check=True)
                        return last
                    wt = [m_wd, (c.cv, m_he[(e, NI - 1)])]
                    if oo >= 2:
                        wt.append((c.cv, (lambda e=e, oo=oo: m_dnev[(e, oo - 2)])))
                    m_dn[(e, oo)] = T(c, f_dn, wait=wt)

                    def f_dnev(v, oo=oo, dp=dp):
                        last = None
                        for s2 in range(NS):
                            sl = slice(oo * CH + s2 * S2, oo * CH + (s2 + 1) * S2)
                            last = v.tensor_tensor(x_sb[:, sl], x_sb[:, sl],
                                                   dp[:, s2 * S2:(s2 + 1) * S2],
                                                   ALU.add)
                        return last
                    m_dnev[(e, oo)] = V(c, f_dnev, wait=[(c.ct, m_dn[(e, oo)])])

            m_done = m_dnev[(experts[-1], NT - 1)]
    return m_done


def emit_attn_front(c, xq_bf, carry, m_ln2, m_carry_prev, outs, chunk,
                    mask_col=None, tcol=None):
    nc = c.nc
    col0 = chunk * CH
    with ExitStack() as ph:
        E = ph.enter_context
        k_bf = E(SB(nc, "at_k", [128, NT * CH], BF16))
        v_bf = E(SB(nc, "at_v", [128, NT * CH], BF16))
        wq_sl = [E(SB(nc, f"at_wq{i}", [128, NT * 128], BF16))
                 for i in range(2)]
        sq_scr = E(SB(nc, "at_sq", [128, CH], F32))
        rq_r = E(SB(nc, "at_rq_r", [1, CH], F32))
        rk_r = E(SB(nc, "at_rk_r", [1, CH], F32))
        rq_b = E(SB(nc, "at_rq_b", [128, CH], F32))
        rk_b = E(SB(nc, "at_rk_b", [128, CH], F32))
        kk_scr = E(SB(nc, "at_kk", [128, CH], F32))
        sigq_scr = [E(SB(nc, f"at_sg{i}", [128, CH], BF16)) for i in range(2)]

        m_qkv = {}
        m_evac = {}
        m_ssq = {}
        m_wsl = {}
        m_sg = {}
        m_sgdma = {}

        with ExitStack() as qscope:
            q_bf = qscope.enter_context(
                SB(nc, "at_q", [128, NT * CH], BF16))
            dsts = [q_bf, k_bf, v_bf]

            with ExitStack() as pp:
                qk_ps = [pp.enter_context(
                    PS(nc, f"at_qk{i}", [128, CH], F32)) for i in range(2)]
                ssq_ps = [pp.enter_context(
                    PS(nc, f"at_ssq{i}", [1, CH], F32)) for i in range(2)]

                for ot in range(3 * NT):
                    p = ot % 2
                    wt = ([(c.ct, (lambda ot=ot: m_qkv[ot - 2]))] if ot >= 2
                          else [(c.cv, m_ln2)])
                    m_wsl[ot] = G(c, f"q{p}", lambda g, ot=ot, p=p: g.dma_start(
                        out=_sb3(wq_sl[p][:], NT),
                        in_=_rearr_rows(c.wqkvT, 0, NT, ot * 128, 128)), wait=wt)

                    def f_mm(t, ot=ot, p=p):
                        last = None
                        for j in range(NT):
                            for s in range(NS):
                                last = t.matmul(qk_ps[p][:, s * S2:(s + 1) * S2],
                                                wq_sl[p][:, j * 128:(j + 1) * 128],
                                                xq_bf[:, j * CH + s * S2:
                                                      j * CH + (s + 1) * S2],
                                                start=(j == 0),
                                                stop=(j == NT - 1),
                                                skip_group_check=True)
                        return last
                    wt = [m_wsl[ot], (c.cv, m_ln2)]
                    if ot >= 2:
                        wt.append((c.cv, (lambda ot=ot: m_evac[ot - 2])))
                    m_qkv[ot] = T(c, f_mm, wait=wt)

                    kind, jt = divmod(ot, NT)
                    m_evac[ot] = V(c, lambda v, p=p, kind=kind, jt=jt:
                                   v.tensor_copy(
                                       dsts[kind][:, jt * CH:(jt + 1) * CH],
                                       qk_ps[p][:]),
                                   wait=[(c.ct, m_qkv[ot])])

                    if kind < 2:
                        m_sq2 = SC(c, lambda s, kind=kind, jt=jt: s.activation(
                            out=sq_scr[:], in_=dsts[kind][:, jt * CH:(jt + 1) * CH],
                            func=AF.Square),
                            wait=[(c.cv, m_evac[ot])] +
                                 ([(c.ct, (lambda ot=ot: m_ssq[ot - 1]))]
                                  if ot >= 1 else []))

                        def f_ssq(t, kind=kind, jt=jt):
                            last = None
                            for s in range(NS):
                                last = t.matmul(
                                    ssq_ps[kind][:, s * S2:(s + 1) * S2],
                                    c.onesc[:], sq_scr[:, s * S2:(s + 1) * S2],
                                    start=(jt == 0), stop=(jt == NT - 1),
                                    skip_group_check=True)
                            return last
                        m_ssq[ot] = T(c, f_ssq, wait=[(c.cs, m_sq2)])

                mrq = V(c, lambda v: v.tensor_scalar_mul(
                    rq_r[:], ssq_ps[0][:], RECIP_D),
                    wait=[(c.ct, m_ssq[2 * NT - 1])])
                m_rows = V(c, lambda v: v.tensor_scalar_mul(
                    rk_r[:], ssq_ps[1][:], RECIP_D), wait=[(c.cv, mrq)])

            def f_rstd(s):
                last = None
                for r in (rq_r, rk_r):
                    s.activation(out=r[:], in_=r[:], func=AF.Ln,
                                 bias=c.c_rmseps[0:1, :])
                    last = s.activation(out=r[:], in_=r[:], func=AF.Exp,
                                        scale=c.c_mhalf[0:1, :])
                return last
            m_rstd = SC(c, f_rstd, wait=[(c.cv, m_rows)])

            with ExitStack() as pp:
                bc_ps = [pp.enter_context(
                    PS(nc, f"at_bc{i}", [128, CH], F32)) for i in range(2)]

                def f_bc(t):
                    last = None
                    for i, r in enumerate((rq_r, rk_r)):
                        for s in range(NS):
                            last = t.matmul(bc_ps[i][:, s * S2:(s + 1) * S2],
                                            c.onesk1[:],
                                            r[:, s * S2:(s + 1) * S2],
                                            start=True, stop=True,
                                            skip_group_check=True)
                    return last
                m_bc = T(c, f_bc, wait=[(c.cs, m_rstd)])

                def f_bcv(v):
                    v.tensor_copy(rq_b[:], bc_ps[0][:])
                    return v.tensor_copy(rk_b[:], bc_ps[1][:])
                m_bcv = V(c, f_bcv, wait=[(c.ct, m_bc)])

            for j in range(NT):
                p = j % 2
                m_qq = V(c, lambda v, j=j: v.tensor_tensor(
                    kk_scr[:], q_bf[:, j * CH:(j + 1) * CH], rq_b[:], ALU.mult),
                    wait=[(c.cv, m_bcv)] +
                         ([(c.cs, (lambda j=j: m_sg[j - 1]))] if j >= 1 else []))
                m_sg[j] = SC(c, lambda s, p=p: s.activation(
                    out=sigq_scr[p][:], in_=kk_scr[:], func=AF.Sigmoid),
                    wait=[(c.cv, m_qq)] +
                         ([(dsem(c, f"sg{p}"),
                            (lambda j=j: m_sgdma[j - 2][1]))] if j >= 2 else []))
                m_sgdma[j] = Y(c, f"sg{p}", lambda y, j=j, p=p: y.dma_start(
                    out=outs["sigq"][j * 128:(j + 1) * 128, col0:col0 + CH],
                    in_=sigq_scr[p][:]), wait=[(c.cs, m_sg[j])])

        w_scr = [E(SB(nc, f"at_w{i}", [128, CH], F32)) for i in range(2)]
        wv_scr = [E(SB(nc, f"at_wv{i}", [128, CH], F32)) for i in range(2)]
        wsc_scr = [E(SB(nc, f"at_wsc{i}", [128, CH], F32)) for i in range(2)]
        wvs_scr = [E(SB(nc, f"at_wvs{i}", [128, CH], F32)) for i in range(2)]

        m_carry = {}
        m_scandma = {}
        m_w = {}
        for j in range(NT):
            p = j % 2
            m_kk = V(c, lambda v, j=j: v.tensor_tensor(
                kk_scr[:], k_bf[:, j * CH:(j + 1) * CH], rk_b[:], ALU.mult),
                wait=[(c.cv, m_bcv)] +
                     ([(c.cs, (lambda j=j: m_w[j - 1]))] if j >= 1 else
                      [(c.cs, m_sg[NT - 1])]))
            m_w[j] = SC(c, lambda s, p=p: s.activation(
                out=w_scr[p][:], in_=kk_scr[:], func=AF.Exp),
                wait=[(c.cv, m_kk), (c.cs, m_sg[NT - 1])] +
                     ([(dsem(c, f"sc{p}"),
                        (lambda j=j: m_scandma[j - 2][1]))] if j >= 2 else []))

            mwv = V(c, lambda v, j=j, p=p: v.tensor_tensor(
                wv_scr[p][:], w_scr[p][:],
                v_bf[:, j * CH:(j + 1) * CH], ALU.mult),
                wait=[(c.cs, m_w[j]), (c.cv, m_carry_prev[j])] +
                     ([(dsem(c, f"sc{p}"),
                        (lambda j=j: m_scandma[j - 2][1]))]
                      if j >= 2 else []))
            msc1 = V(c, lambda v, j=j, p=p: v.tensor_tensor_scan(
                out=wsc_scr[p][:], data0=c.ones_t[:], data1=w_scr[p][:],
                initial=carry[:, j:j + 1], op0=ALU.mult, op1=ALU.add),
                wait=[(c.cv, mwv)])
            msc2 = V(c, lambda v, j=j, p=p: v.tensor_tensor_scan(
                out=wvs_scr[p][:], data0=c.ones_t[:], data1=wv_scr[p][:],
                initial=carry[:, 16 + j:17 + j], op0=ALU.mult, op1=ALU.add),
                wait=[(c.cv, msc1)])

            if mask_col is None:
                def f_cupd(v, j=j, p=p):
                    v.tensor_copy(carry[:, j:j + 1], wsc_scr[p][:, CH - 1:CH])
                    return v.tensor_copy(carry[:, 16 + j:17 + j],
                                         wvs_scr[p][:, CH - 1:CH])
                m_carry[j] = V(c, f_cupd, wait=[(c.cv, msc2)])
            else:
                def f_cd(v, j=j, p=p):
                    v.tensor_tensor(tcol[:, 0:1], wsc_scr[p][:, CH - 1:CH],
                                    carry[:, j:j + 1], ALU.subtract)
                    return v.tensor_tensor(tcol[:, 1:2],
                                           wvs_scr[p][:, CH - 1:CH],
                                           carry[:, 16 + j:17 + j],
                                           ALU.subtract)
                mcd = V(c, f_cd, wait=[(c.cv, msc2)])
                mc1 = V(c, lambda v, j=j: v.scalar_tensor_tensor(
                    carry[:, j:j + 1], tcol[:, 0:1], mask_col,
                    carry[:, j:j + 1], ALU.mult, ALU.add),
                    wait=[(c.cv, mcd)])
                m_carry[j] = V(c, lambda v, j=j: v.scalar_tensor_tensor(
                    carry[:, 16 + j:17 + j], tcol[:, 1:2], mask_col,
                    carry[:, 16 + j:17 + j], ALU.mult, ALU.add),
                    wait=[(c.cv, mc1)])

            Y(c, f"sc{p}", lambda y, j=j, p=p: y.dma_start(
                out=outs["wscan"][j * 128:(j + 1) * 128, col0:col0 + CH],
                in_=wsc_scr[p][:]), wait=[(c.cv, m_carry[j])])
            m_scandma[j] = Y(c, f"sc{p}", lambda y, j=j, p=p: y.dma_start(
                out=outs["wvscan"][j * 128:(j + 1) * 128, col0:col0 + CH],
                in_=wvs_scr[p][:]))
    return m_carry, m_scandma[NT - 1]


def build_l1(stage="full"):
    nc = bass.Bass("TRN2", target_bir_lowering=False,
                   detect_race_conditions=False)
    xc = nc.dram_tensor("xc", [D, TCORE], F32, kind="ExternalInput").ap()
    wqkvT = nc.dram_tensor("wqkvT", [D, 6144], F32, kind="ExternalInput").ap()
    gateT = nc.dram_tensor("gateT", [NE, D, DI], F32, kind="ExternalInput").ap()
    upT = nc.dram_tensor("upT", [NE, D, DI], F32, kind="ExternalInput").ap()
    downT = nc.dram_tensor("downT", [NE, DI, D], F32, kind="ExternalInput").ap()
    routerT = nc.dram_tensor("routerT", [D, 8], F32, kind="ExternalInput").ap()
    ident_in = nc.dram_tensor("ident", [128, 128], F32, kind="ExternalInput").ap()
    sel8_in = nc.dram_tensor("sel8", [8, 8 * 128], F32, kind="ExternalInput").ap()

    xpart = nc.dram_tensor("xpart", [D, TCORE], F32, kind="ExternalOutput").ap()
    sigq_o = nc.dram_tensor("sigq", [D, TCORE], BF16, kind="ExternalOutput").ap()
    wscan_o = nc.dram_tensor("wscan", [D, TCORE], F32, kind="ExternalOutput").ap()
    wvscan_o = nc.dram_tensor("wvscan", [D, TCORE], F32,
                              kind="ExternalOutput").ap()
    totals_o = nc.dram_tensor("totals", [128, 32], F32, kind="ExternalOutput").ap()
    outs = {"sigq": sigq_o, "wscan": wscan_o, "wvscan": wvscan_o}

    with ExitStack() as stack:
        c = make_ctx(nc, stack)
        c.gateT = [gateT[e] for e in range(NE)]
        c.upT = [upT[e] for e in range(NE)]
        c.downT = [downT[e] for e in range(NE)]
        c.wqkvT = wqkvT
        c.ident = sbuf(c, "ident_sb", [128, 128], F32)
        c.sel8 = sbuf(c, "sel8_sb", [8, 8 * 128], F32)
        c.routerT_bf = sbuf(c, "routerT_bf", [128, NT * 8], BF16)
        carry = sbuf(c, "carry", [128, 32], F32)

        emit_consts(c)
        Y(c, "cst", lambda y: y.dma_start(out=c.ident[:], in_=ident_in))
        c.m_cst = Y(c, "cst", lambda y: y.dma_start(out=c.sel8[:], in_=sel8_in))
        c.m_router = G(c, "rt", lambda g: g.dma_start(
            out=_sb3(c.routerT_bf[:], NT), in_=_rearr_rows(routerT, 0, NT, 0, 8)))
        m_cinit = V(c, lambda v: v.memset(carry[:], 0.0), wait=[c.m_cst])

        m_carry_prev = {j: m_cinit for j in range(NT)}
        m_prev_done = None
        m_prev_spill = None
        for ch in range(NCH):
            with ExitStack() as chs:
                with ExitStack() as xscope:
                    x_sb = xscope.enter_context(
                        SB(nc, f"x_sb{ch}", [128, NT * CH], F32,
                                       side="right"))

                    wt = [] if m_prev_spill is None else [m_prev_spill]
                    m_xload = Y(c, "x", lambda y, x_sb=x_sb, ch=ch: y.dma_start(
                        out=_sb3(x_sb[:], NT),
                        in_=_rearr_rows(xc, 0, NT, ch * CH, CH)),
                        wait=wt)

                    deps = [m_xload, (c.cv, m_cinit)]
                    if m_prev_done is not None:
                        deps.append((c.cv, m_prev_done))
                        deps.append(m_prev_ld)
                    if stage == "copy":
                        m_f1 = None
                        m_last = deps
                    elif stage == "ln1":
                        xq_bf = chs.enter_context(
                            SB(nc, f"xq{ch}", [128, NT * CH], BF16))
                        m_f1 = emit_ln(c, x_sb, attn=False, resid_scale=0.5,
                                       xout_bf=xq_bf, deps=deps)
                        m_last = [(c.cv, m_f1)]
                    elif stage == "router":
                        xq0 = chs.enter_context(
                            SB(nc, f"xq{ch}", [128, NT * CH], BF16))
                        cmb = chs.enter_context(SB(nc, f"cmb{ch}", [8, CH], F32))
                        c.dbg_w8 = outs["wvscan"][128:256, 0:16]
                        c.dbg_w8_sb = chs.enter_context(
                            SB(nc, f"dbgw8{ch}", [128, 16], F32))
                        m_lnr = emit_ln(c, x_sb, attn=False, resid_scale=0.5,
                                        xout_bf=xq0, deps=deps)
                        m_rt = emit_router(
                            c, xq0, cmb, m_lnr,
                            dbg_out=outs["wvscan"][0:8, ch * CH:(ch + 1) * CH])
                        Y(c, "sc0", lambda y, cmb=cmb, ch=ch: y.dma_start(
                            out=outs["wscan"][0:8, ch * CH:(ch + 1) * CH],
                            in_=cmb[:]), wait=[(c.cv, m_rt)])
                        m_f1 = m_lnr
                        m_last = [(c.cv, m_rt)]
                    else:
                        exps = None
                        if stage == "shared":
                            exps = [8]
                        elif stage == "exp0":
                            exps = [0]
                        m_f1 = emit_ffn(c, x_sb, 0.5, deps, tag=f"f1c{ch}",
                                        experts=exps)
                        m_last = [(c.cv, m_f1)]
                    if stage in ("ln2", "full"):
                        xq_bf = chs.enter_context(
                            SB(nc, f"xq{ch}", [128, NT * CH], BF16))
                        m_ln2 = emit_ln(c, x_sb, attn=True, resid_scale=1.0,
                                        xout_bf=xq_bf, deps=[(c.cv, m_f1)])
                        m_last = [(c.cv, m_ln2)]
                    else:
                        m_ln2 = None
                    m_xspill = Y(c, "x", lambda y, x_sb=x_sb, ch=ch: y.dma_start(
                        out=_rearr_rows(xpart, 0, NT, ch * CH, CH),
                        in_=_sb3(x_sb[:], NT)),
                        wait=list(m_last))
                    m_prev_spill = m_xspill

                if stage == "full":
                    m_carry_prev, m_prev_ld = emit_attn_front(
                        c, xq_bf, carry, m_ln2, m_carry_prev, outs, ch)
                    m_prev_done = m_carry_prev[NT - 1]
                else:
                    m_prev_done = m_cinit
                    m_prev_ld = m_xspill

        m_tot = Y(c, "x", lambda y: y.dma_start(out=totals_o, in_=carry[:]),
                  wait=[(c.cv, m_carry_prev[NT - 1])])

        def f_fin(y):
            return y.nop()
        fin_waits = [(sem, sem.n) for sem in c._dsems.values()]
        _record(c, "sync", c.ct, f_fin, wait=fin_waits, inc=0)

        with nc.Block() as block:
            replay(c, block)
    return nc


def emit_attn_back(c, x_sb, y_bf, carryP, m_cp, ins, chunk, deps):
    nc = c.nc
    col0 = chunk * CH
    with ExitStack() as ph:
        E = ph.enter_context
        ws_b = [E(SB(nc, f"b2w{i}", [128, CH], F32)) for i in range(2)]
        wv_b = [E(SB(nc, f"b2v{i}", [128, CH], F32)) for i in range(2)]
        sq_b = [E(SB(nc, f"b2s{i}", [128, CH], BF16)) for i in range(2)]
        scrW = [E(SB(nc, f"b2scrW{i}", [128, CH], F32)) for i in range(2)]
        scrK = E(SB(nc, "b2scrK", [128, CH], F32))
        scrY = E(SB(nc, "b2scrY", [128, CH], F32))
        recW = [E(SB(nc, f"b2rec{i}", [128, CH], F32)) for i in range(2)]

        m_y = {}
        m_rec = {}
        for j in range(NT):
            p = j % 2
            war = ([(c.cv, (lambda j=j: m_y[j - 2]))] if j >= 2 else list(deps))
            m_dw = Y(c, f"aw{p}", lambda y, j=j, p=p: y.dma_start(
                out=ws_b[p][:],
                in_=ins["wscan"][j * 128:(j + 1) * 128, col0:col0 + CH]),
                wait=war)
            m_dv = Y(c, f"av{p}", lambda y, j=j, p=p: y.dma_start(
                out=wv_b[p][:],
                in_=ins["wvscan"][j * 128:(j + 1) * 128, col0:col0 + CH]),
                wait=war)
            m_ds = Y(c, f"as{p}", lambda y, j=j, p=p: y.dma_start(
                out=sq_b[p][:],
                in_=ins["sigq"][j * 128:(j + 1) * 128, col0:col0 + CH]),
                wait=war)

            m_wge = V(c, lambda v, j=j, p=p: v.tensor_scalar(
                scrW[p][:], ws_b[p][:], carryP[:, j:j + 1], 1e-6,
                ALU.add, ALU.add),
                wait=[m_dw, (c.cv, m_cp)] +
                     ([(c.cs, (lambda j=j: m_rec[j - 2]))] if j >= 2 else []))

            def f_rec(s, p=p):
                s.activation(out=recW[p][:], in_=scrW[p][:], func=AF.Ln)
                return s.activation(out=recW[p][:], in_=recW[p][:], func=AF.Exp,
                                    scale=c.c_m1[:])
            m_rec[j] = SC(c, f_rec, wait=[(c.cv, m_wge)] +
                          ([(c.cv, (lambda j=j: m_y[j - 2]))] if j >= 2 else []))

            m_kvg = V(c, lambda v, j=j, p=p: v.tensor_scalar_add(
                scrK[:], wv_b[p][:], carryP[:, 16 + j:17 + j]),
                wait=[m_dv, (c.cv, m_cp)])

            def f_y(v, j=j, p=p):
                v.tensor_tensor(scrY[:], scrK[:], recW[p][:], ALU.mult)
                return v.tensor_tensor(y_bf[:, j * CH:(j + 1) * CH], scrY[:],
                                       sq_b[p][:], ALU.mult)
            m_y[j] = V(c, f_y, wait=[(c.cv, m_kvg), (c.cs, m_rec[j]), m_ds])
    return m_y[NT - 1]


def emit_swiglu_out(c, x_sb, y_bf, m_y, deps_ps, deps_x, tag):
    nc = c.nc
    with ExitStack() as ph:
        E = ph.enter_context
        h_bf = E(SB(nc, f"{tag}_h", [128, 16 * CH], BF16))
        wu_sl = [E(SB(nc, f"{tag}_wu{i}", [128, NT * 128], BF16)) for i in range(2)]
        wg_sl = [E(SB(nc, f"{tag}_wg{i}", [128, NT * 128], BF16)) for i in range(2)]
        wo_sl = [E(SB(nc, f"{tag}_wo{i}", [128, 16 * 128], BF16)) for i in range(4)]
        sgs = [E(SB(nc, f"{tag}_sg{i}", [128, S2], F32)) for i in range(2)]
        silu = E(SB(nc, f"{tag}_si", [128, S2], F32))

        m_ug = {}
        m_h = {}
        m_op = {}
        m_opev = {}
        m_wu = {}
        m_wo = {}
        for gr in range(4):
            with ExitStack() as pp:
                u_ps = [pp.enter_context(PS(nc, f"{tag}_u{gr}{i}", [128, CH], F32))
                        for i in range(2)]
                g_ps = [pp.enter_context(PS(nc, f"{tag}_g{gr}{i}", [128, CH], F32))
                        for i in range(2)]
                for ff in range(16):
                    ffg = gr * 16 + ff
                    p = ff % 2
                    war = ([(c.ct, (lambda gr=gr, ff=ff: m_ug[(gr, ff - 2)]))]
                           if ff >= 2 else
                           ([(c.ct, (lambda gr=gr, ff=ff: m_op[(gr - 1, NT - 2 + ff)]))]
                            if gr >= 1 else [(c.cv, m_y)]))
                    mu = G(c, f"su{p}", lambda g, ffg=ffg, p=p: g.dma_start(
                        out=_sb3(wu_sl[p][:], NT),
                        in_=_rearr_rows(c.wswT, 0, NT, ffg * 128, 128)),
                        wait=war)
                    mg = G(c, f"sx{p}", lambda g, ffg=ffg, p=p: g.dma_start(
                        out=_sb3(wg_sl[p][:], NT),
                        in_=_rearr_rows(c.wswT, 0, NT, DFF + ffg * 128, 128)),
                        wait=war)

                    def f_ug(t, ff=ff, p=p):
                        last = None
                        for j in range(NT):
                            for s in range(NS):
                                t.matmul(u_ps[p][:, s * S2:(s + 1) * S2],
                                         wu_sl[p][:, j * 128:(j + 1) * 128],
                                         y_bf[:, j * CH + s * S2:
                                              j * CH + (s + 1) * S2],
                                         start=(j == 0), stop=(j == NT - 1),
                                         skip_group_check=True)
                        for j in range(NT):
                            for s in range(NS):
                                last = t.matmul(
                                    g_ps[p][:, s * S2:(s + 1) * S2],
                                    wg_sl[p][:, j * 128:(j + 1) * 128],
                                    y_bf[:, j * CH + s * S2:
                                         j * CH + (s + 1) * S2],
                                    start=(j == 0), stop=(j == NT - 1),
                                    skip_group_check=True)
                        return last
                    wt = [mu, mg, (c.cv, m_y)]
                    prev = ((gr, ff - 2) if ff >= 2 else
                            ((gr - 1, 14 + ff) if gr >= 1 else None))
                    if prev is not None:
                        wt.append((c.cv, (lambda prev=prev: m_h[prev])))
                    if gr == 0 and ff == 0:
                        wt += list(deps_ps)
                    if ff <= 1 and gr >= 1:
                        wt.append((c.cv, (lambda gr=gr: m_opev[(gr - 1, NT - 1)])))
                    m_ug[(gr, ff)] = T(c, f_ug, wait=wt)

                    def f_sg(s, p=p):
                        last = None
                        for s2 in range(NS):
                            last = s.activation(
                                out=sgs[s2][:],
                                in_=g_ps[p][:, s2 * S2:(s2 + 1) * S2],
                                func=AF.Sigmoid)
                        return last
                    wt_s = [(c.ct, m_ug[(gr, ff)])]
                    if prev is not None:
                        wt_s.append((c.cv, (lambda prev=prev: m_h[prev])))
                    m_sg = SC(c, f_sg, wait=wt_s)

                    def f_h(v, ff=ff, p=p):
                        last = None
                        for s2 in range(NS):
                            v.tensor_tensor(silu[:], sgs[s2][:],
                                            g_ps[p][:, s2 * S2:(s2 + 1) * S2],
                                            ALU.mult)
                            last = v.tensor_tensor(
                                h_bf[:, ff * CH + s2 * S2:
                                     ff * CH + (s2 + 1) * S2],
                                silu[:], u_ps[p][:, s2 * S2:(s2 + 1) * S2],
                                ALU.mult)
                        return last
                    m_h[(gr, ff)] = V(c, f_h, wait=[(c.cs, m_sg)])

            with ExitStack() as pp:
                op_ps = [pp.enter_context(PS(nc, f"{tag}_o{gr}{i}", [128, CH], F32))
                         for i in range(2)]
                for oo in range(NT):
                    p = oo % 2
                    p4 = oo % 4
                    war = ([(c.ct, (lambda gr=gr, oo=oo: m_op[(gr, oo - 4)]))]
                           if oo >= 4 else
                           ([(c.ct, (lambda gr=gr, oo=oo: m_ug[(gr, 12 + oo)]))]
                            if True else []))
                    m_wo[(gr, oo)] = G(c, f"wo{p4}", lambda g, gr=gr, oo=oo, p4=p4:
                                       g.dma_start(
                        out=_sb3(wo_sl[p4][:], 16),
                        in_=_rearr_rows(c.woutT, gr * 2048, 16, oo * 128, 128)),
                        wait=war)

                    def f_op(t, gr=gr, oo=oo, p=p, p4=p4):
                        last = None
                        for k in range(16):
                            for s in range(NS):
                                last = t.matmul(
                                    op_ps[p][:, s * S2:(s + 1) * S2],
                                    wo_sl[p4][:, k * 128:(k + 1) * 128],
                                    h_bf[:, k * CH + s * S2:
                                         k * CH + (s + 1) * S2],
                                    start=(k == 0), stop=(k == 15),
                                    skip_group_check=True)
                        return last
                    wt = [m_wo[(gr, oo)], (c.cv, m_h[(gr, 15)])]
                    prevv = ((gr, oo - 2) if oo >= 2 else
                             ((gr - 1, NT - 2 + oo) if gr >= 1 else None))
                    if prevv is not None:
                        wt.append((c.cv, (lambda prevv=prevv: m_opev[prevv])))
                    if oo < 2:
                        wt.append((c.cv, m_h[(gr, 15)]))
                    m_op[(gr, oo)] = T(c, f_op, wait=wt)

                    def f_opev(v, oo=oo, p=p):
                        last = None
                        for s2 in range(NS):
                            sl = slice(oo * CH + s2 * S2, oo * CH + (s2 + 1) * S2)
                            last = v.tensor_tensor(
                                x_sb[:, sl], x_sb[:, sl],
                                op_ps[p][:, s2 * S2:(s2 + 1) * S2], ALU.add)
                        return last
                    wt_ev = [(c.ct, m_op[(gr, oo)])]
                    if gr == 0 and oo == 0:
                        wt_ev += list(deps_x)
                    m_opev[(gr, oo)] = V(c, f_opev, wait=wt_ev)
        m_done = m_opev[(3, NT - 1)]
    return m_done


def build_l2():
    nc = bass.Bass("TRN2", target_bir_lowering=False,
                   detect_race_conditions=False)
    xpart = nc.dram_tensor("xpart", [D, TCORE], F32, kind="ExternalInput").ap()
    sigq_i = nc.dram_tensor("sigq", [D, TCORE], BF16, kind="ExternalInput").ap()
    wscan_i = nc.dram_tensor("wscan", [D, TCORE], F32, kind="ExternalInput").ap()
    wvscan_i = nc.dram_tensor("wvscan", [D, TCORE], F32,
                              kind="ExternalInput").ap()
    alltot = nc.dram_tensor("alltot", [8 * 128, 32], F32,
                            kind="ExternalInput").ap()
    predw = nc.dram_tensor("predw", [128, 8], F32, kind="ExternalInput").ap()
    wswT = nc.dram_tensor("wswT", [D, 2 * DFF], F32, kind="ExternalInput").ap()
    woutT = nc.dram_tensor("woutT", [DFF, D], F32, kind="ExternalInput").ap()
    gateT = nc.dram_tensor("gateT", [NE, D, DI], F32, kind="ExternalInput").ap()
    upT = nc.dram_tensor("upT", [NE, D, DI], F32, kind="ExternalInput").ap()
    downT = nc.dram_tensor("downT", [NE, DI, D], F32, kind="ExternalInput").ap()
    routerT = nc.dram_tensor("routerT", [D, 8], F32, kind="ExternalInput").ap()
    ident_in = nc.dram_tensor("ident", [128, 128], F32, kind="ExternalInput").ap()
    sel8_in = nc.dram_tensor("sel8", [8, 8 * 128], F32, kind="ExternalInput").ap()
    out_o = nc.dram_tensor("out", [D, TCORE], F32, kind="ExternalOutput").ap()
    ins = {"sigq": sigq_i, "wscan": wscan_i, "wvscan": wvscan_i}

    with ExitStack() as stack:
        c = make_ctx(nc, stack)
        c.gateT = [gateT[e] for e in range(NE)]
        c.upT = [upT[e] for e in range(NE)]
        c.downT = [downT[e] for e in range(NE)]
        c.wswT = wswT
        c.woutT = woutT
        c.ident = sbuf(c, "ident_sb", [128, 128], F32)
        c.sel8 = sbuf(c, "sel8_sb", [8, 8 * 128], F32)
        c.routerT_bf = sbuf(c, "routerT_bf", [128, NT * 8], BF16)
        carryP = sbuf(c, "carryP", [128, 32], F32)

        emit_consts(c)
        Y(c, "cst", lambda y: y.dma_start(out=c.ident[:], in_=ident_in))
        c.m_cst = Y(c, "cst", lambda y: y.dma_start(out=c.sel8[:], in_=sel8_in))
        c.m_router = G(c, "rt", lambda g: g.dma_start(
            out=_sb3(c.routerT_bf[:], NT), in_=_rearr_rows(routerT, 0, NT, 0, 8)))
        with ExitStack() as tsc:
            tots = tsc.enter_context(SB(nc, "tots", [128, 8 * 32], F32))
            pw_sb = tsc.enter_context(SB(nc, "pw_sb", [128, 8], F32))
            Y(c, "tot", lambda y: y.dma_start(
                out=tots[:].rearrange("p (a x) -> p a x", a=8),
                in_=_rearr_rows(alltot, 0, 8, 0, 32)))
            m_tot = Y(c, "tot", lambda y: y.dma_start(out=pw_sb[:], in_=predw))

            m_cp = V(c, lambda v: v.memset(carryP[:], 0.0), wait=[m_tot])
            for i in range(8):
                m_cp = V(c, lambda v, i=i: v.scalar_tensor_tensor(
                    carryP[:], tots[:, i * 32:(i + 1) * 32], pw_sb[:, i:i + 1],
                    carryP[:], ALU.mult, ALU.add), wait=[(c.cv, m_cp)])

        m_prev_done = None
        m_prev_out = None
        for ch in range(NCH):
            with ExitStack() as chs:
                with ExitStack() as xscope:
                    x_sb = xscope.enter_context(
                        SB(nc, f"x2_{ch}", [128, NT * CH], F32, side="right"))

                    wt = [] if m_prev_out is None else [m_prev_out]
                    m_xload = Y(c, "x", lambda y, x_sb=x_sb, ch=ch: y.dma_start(
                        out=_sb3(x_sb[:], NT),
                        in_=_rearr_rows(xpart, 0, NT, ch * CH, CH)),
                        wait=wt)

                    deps = [(c.cv, m_cp)]
                    if m_prev_done is not None:
                        deps.append((c.cv, m_prev_done))
                    with ExitStack() as yscope:
                        y_bf = yscope.enter_context(
                            SB(nc, f"y2_{ch}", [128, NT * CH], BF16))
                        m_y = emit_attn_back(c, x_sb, y_bf, carryP, m_cp, ins,
                                             ch, deps)
                        deps_ps = [(c.cv, m_prev_done)] if m_prev_done else []
                        m_sw = emit_swiglu_out(c, x_sb, y_bf, m_y, deps_ps,
                                               [m_xload], tag=f"sw{ch}")
                    m_f2 = emit_ffn(c, x_sb, 0.5,
                                    [(c.cv, m_sw), m_xload], tag=f"f2c{ch}")
                    m_prev_out = Y(c, "x", lambda y, x_sb=x_sb, ch=ch:
                                   y.dma_start(
                        out=_rearr_rows(out_o, 0, NT, ch * CH, CH),
                        in_=_sb3(x_sb[:], NT)), wait=[(c.cv, m_f2)])
                    m_prev_done = m_f2

        def f_fin(y):
            return y.nop()
        fin_waits = [(sem, sem.n) for sem in c._dsems.values()]
        _record(c, "sync", c.ct, f_fin, wait=fin_waits, inc=0)

        with nc.Block() as block:
            replay(c, block)
    return nc



def _host_weights(inputs):
    asc = np.ascontiguousarray
    f32 = lambda a: np.asarray(a, np.float32)
    wqkvT = asc(f32(inputs["L0_attn_w_qkv"]).T)
    wswT = asc(f32(inputs["L0_attn_w_swiglu"]).T)
    woutT = asc(f32(inputs["L0_attn_w_out"]).T)
    routerT = asc(f32(inputs["L0_ffn_router_w"]).T)
    wg = f32(inputs["L0_ffn_w_gate"])
    wu = f32(inputs["L0_ffn_w_up"])
    wd = f32(inputs["L0_ffn_w_down"])
    sg = f32(inputs["L0_ffn_shared_gate"])
    su = f32(inputs["L0_ffn_shared_up"])
    sd = f32(inputs["L0_ffn_shared_down"])
    gateT = asc(np.concatenate([wg.transpose(0, 2, 1), sg.T[None]], axis=0))
    upT = asc(np.concatenate([wu.transpose(0, 2, 1), su.T[None]], axis=0))
    downT = asc(np.concatenate([wd.transpose(0, 2, 1), sd.T[None]], axis=0))
    ident = np.eye(128, dtype=np.float32)
    sel8 = np.zeros((8, 8 * 128), np.float32)
    for e in range(8):
        sel8[e, e * 128:(e + 1) * 128] = 1.0
    return dict(wqkvT=wqkvT, wswT=wswT, woutT=woutT, routerT=routerT,
                gateT=gateT, upT=upT, downT=downT, ident=ident, sel8=sel8)


def kernel(**inputs):
    W = _host_weights(inputs)
    x = np.asarray(inputs["L0_attn_x"], np.float32)

    in1 = []
    for core in range(8):
        b, q = divmod(core, 4)
        xc = np.ascontiguousarray(x[b, q * TCORE:(q + 1) * TCORE, :].T)
        in1.append(dict(xc=xc, wqkvT=W["wqkvT"], gateT=W["gateT"],
                        upT=W["upT"], downT=W["downT"], routerT=W["routerT"],
                        ident=W["ident"], sel8=W["sel8"]))
    nc1 = build_l1()
    res1 = run_bass_kernel_spmd(nc1, in1, core_ids=list(range(8)))
    o1 = res1.results

    alltot = np.ascontiguousarray(
        np.concatenate([o1[cc]["totals"] for cc in range(8)], axis=0))
    in2 = []
    for core in range(8):
        b, q = divmod(core, 4)
        predw = np.zeros((128, 8), np.float32)
        for i in range(b * 4, core):
            predw[:, i] = 1.0
        in2.append(dict(xpart=o1[core]["xpart"], sigq=o1[core]["sigq"],
                        wscan=o1[core]["wscan"], wvscan=o1[core]["wvscan"],
                        alltot=alltot, predw=predw, wswT=W["wswT"],
                        woutT=W["woutT"], gateT=W["gateT"], upT=W["upT"],
                        downT=W["downT"], routerT=W["routerT"],
                        ident=W["ident"], sel8=W["sel8"]))
    nc2 = build_l2()
    res2 = run_bass_kernel_spmd(nc2, in2, core_ids=list(range(8)))

    out = np.empty((B, S, D), np.float32)
    for core in range(8):
        b, q = divmod(core, 4)
        out[b, q * TCORE:(q + 1) * TCORE, :] = res2.results[core]["out"].T
    return out


def build_fused():
    NCHF = 8
    nc = bass.Bass("TRN2", target_bir_lowering=False,
                   detect_race_conditions=False)
    xin = nc.dram_tensor("xin", [D, NCHF * CH], F32, kind="ExternalInput").ap()
    cmask = nc.dram_tensor("cmask", [128, NCHF], F32, kind="ExternalInput").ap()
    wqkvT = nc.dram_tensor("wqkvT", [D, 6144], F32, kind="ExternalInput").ap()
    wswT = nc.dram_tensor("wswT", [D, 2 * DFF], F32, kind="ExternalInput").ap()
    woutT = nc.dram_tensor("woutT", [DFF, D], F32, kind="ExternalInput").ap()
    gateT = nc.dram_tensor("gateT", [NE, D, DI], F32, kind="ExternalInput").ap()
    upT = nc.dram_tensor("upT", [NE, D, DI], F32, kind="ExternalInput").ap()
    downT = nc.dram_tensor("downT", [NE, DI, D], F32, kind="ExternalInput").ap()
    routerT = nc.dram_tensor("routerT", [D, 8], F32, kind="ExternalInput").ap()
    ident_in = nc.dram_tensor("ident", [128, 128], F32, kind="ExternalInput").ap()
    sel8_in = nc.dram_tensor("sel8", [8, 8 * 128], F32, kind="ExternalInput").ap()
    out_o = nc.dram_tensor("out", [D, TCORE], F32, kind="ExternalOutput").ap()

    sp_x = nc.dram_tensor("sp_x", [D, NCHF * CH], F32, kind="Internal").ap()
    sp_sq = nc.dram_tensor("sp_sq", [D, NCHF * CH], BF16, kind="Internal").ap()
    sp_w = nc.dram_tensor("sp_w", [D, NCHF * CH], F32, kind="Internal").ap()
    sp_wv = nc.dram_tensor("sp_wv", [D, NCHF * CH], F32, kind="Internal").ap()
    outs = {"sigq": sp_sq, "wscan": sp_w, "wvscan": sp_wv}

    with ExitStack() as stack:
        c = make_ctx(nc, stack)
        c.gateT = [gateT[e] for e in range(NE)]
        c.upT = [upT[e] for e in range(NE)]
        c.downT = [downT[e] for e in range(NE)]
        c.wqkvT = wqkvT
        c.wswT = wswT
        c.woutT = woutT
        c.ident = sbuf(c, "ident_sb", [128, 128], F32)
        c.sel8 = sbuf(c, "sel8_sb", [8, 8 * 128], F32)
        c.routerT_bf = sbuf(c, "routerT_bf", [128, NT * 8], BF16)
        carry = sbuf(c, "carry", [128, 32], F32)
        carryP = sbuf(c, "carryP", [128, 32], F32)
        mask_sb = sbuf(c, "mask_sb", [128, NCHF], F32)
        tcol = sbuf(c, "tcol", [128, 2], F32)

        emit_consts(c)
        Y(c, "cst", lambda y: y.dma_start(out=c.ident[:], in_=ident_in))
        Y(c, "cst", lambda y: y.dma_start(out=c.sel8[:], in_=sel8_in))
        c.m_cst = Y(c, "cst", lambda y: y.dma_start(out=mask_sb[:], in_=cmask))
        c.m_router = G(c, "rt", lambda g: g.dma_start(
            out=_sb3(c.routerT_bf[:], NT), in_=_rearr_rows(routerT, 0, NT, 0, 8)))

        def f_ci(v):
            v.memset(carry[:], 0.0)
            return v.memset(carryP[:], 0.0)
        m_cinit = V(c, f_ci, wait=[c.m_cst])

        m_carry_prev = {j: m_cinit for j in range(NT)}
        m_prev_done = None
        m_prev_spill = None
        m_prev_ld = None
        for ch in range(NCHF):
            with ExitStack() as chs:
                with ExitStack() as xscope:
                    x_sb = xscope.enter_context(
                        SB(nc, f"xf{ch}", [128, NT * CH], F32, side="right"))
                    wt = [] if m_prev_spill is None else [m_prev_spill]
                    m_xload = Y(c, "x", lambda y, x_sb=x_sb, ch=ch: y.dma_start(
                        out=_sb3(x_sb[:], NT),
                        in_=_rearr_rows(xin, 0, NT, ch * CH, CH)), wait=wt)

                    deps = [m_xload, (c.cv, m_cinit)]
                    if m_prev_done is not None:
                        deps.append((c.cv, m_prev_done))
                        deps.append(m_prev_ld)
                    m_f1 = emit_ffn(c, x_sb, 0.5, deps, tag=f"af{ch}")
                    xq_bf = chs.enter_context(
                        SB(nc, f"xqf{ch}", [128, NT * CH], BF16))
                    m_ln2 = emit_ln(c, x_sb, attn=True, resid_scale=1.0,
                                    xout_bf=xq_bf, deps=[(c.cv, m_f1)])
                    m_prev_spill = Y(c, "x", lambda y, x_sb=x_sb, ch=ch:
                                     y.dma_start(
                        out=_rearr_rows(sp_x, 0, NT, ch * CH, CH),
                        in_=_sb3(x_sb[:], NT)), wait=[(c.cv, m_ln2)])

                m_carry_prev, m_prev_ld = emit_attn_front(
                    c, xq_bf, carry, m_ln2, m_carry_prev, outs, ch,
                    mask_col=mask_sb[:, ch:ch + 1], tcol=tcol)
                m_prev_done = m_carry_prev[NT - 1]

        ins = {"sigq": sp_sq, "wscan": sp_w, "wvscan": sp_wv}
        m_prev_out = None
        for ci, ch in enumerate((NCHF - 2, NCHF - 1)):
            with ExitStack() as chs:
                with ExitStack() as xscope:
                    x_sb = xscope.enter_context(
                        SB(nc, f"xb{ch}", [128, NT * CH], F32, side="right"))
                    wt = [m_prev_spill] if m_prev_out is None else [m_prev_out]
                    m_xload = Y(c, "x", lambda y, x_sb=x_sb, ch=ch: y.dma_start(
                        out=_sb3(x_sb[:], NT),
                        in_=_rearr_rows(sp_x, 0, NT, ch * CH, CH)), wait=wt)

                    deps = [(c.cv, m_cinit), (c.cv, m_prev_done)]
                    for nm in ("sc0", "sc1", "sg0", "sg1"):
                        sm = dsem(c, nm)
                        deps.append((sm, sm.n))
                    with ExitStack() as yscope:
                        y_bf = yscope.enter_context(
                            SB(nc, f"yb{ch}", [128, NT * CH], BF16))
                        m_y = emit_attn_back(c, x_sb, y_bf, carryP, m_cinit,
                                             ins, ch, deps)
                        m_sw = emit_swiglu_out(c, x_sb, y_bf, m_y,
                                               [(c.cv, m_prev_done)],
                                               [m_xload], tag=f"swf{ch}")
                    m_f2 = emit_ffn(c, x_sb, 0.5, [(c.cv, m_sw), m_xload],
                                    tag=f"bf{ch}")
                    m_prev_out = Y(c, "x", lambda y, x_sb=x_sb, ci=ci:
                                   y.dma_start(
                        out=_rearr_rows(out_o, 0, NT, ci * CH, CH),
                        in_=_sb3(x_sb[:], NT)), wait=[(c.cv, m_f2)])
                    m_prev_done = m_f2

        def f_fin(y):
            return y.nop()
        fin_waits = [(sem, sem.n) for sem in c._dsems.values()]
        _record(c, "sync", c.ct, f_fin, wait=fin_waits, inc=0)

        with nc.Block() as block:
            replay(c, block)
    return nc


def kernel_fused(**inputs):
    W = _host_weights(inputs)
    x = np.asarray(inputs["L0_attn_x"], np.float32)

    in_maps = []
    for core in range(8):
        b, q = divmod(core, 4)
        npad = 3 - q
        xin = np.zeros((D, 8 * CH), np.float32)
        real = np.ascontiguousarray(x[b, 0:(q + 1) * TCORE, :].T)
        xin[:, npad * TCORE:] = real
        cmask = np.zeros((128, 8), np.float32)
        cmask[:, npad * 2:] = 1.0
        in_maps.append(dict(
            xin=xin, cmask=cmask, wqkvT=W["wqkvT"], wswT=W["wswT"],
            woutT=W["woutT"], gateT=W["gateT"], upT=W["upT"],
            downT=W["downT"], routerT=W["routerT"], ident=W["ident"],
            sel8=W["sel8"]))
    nc = build_fused()
    res = run_bass_kernel_spmd(nc, in_maps, core_ids=list(range(8)))

    out = np.empty((B, S, D), np.float32)
    for core in range(8):
        b, q = divmod(core, 4)
        out[b, q * TCORE:(q + 1) * TCORE, :] = res.results[core]["out"].T
    return out
